# version 02 (final)
# baseline (speedup 1.0000x reference)
import numpy as np
import ml_dtypes
from contextlib import ExitStack

import concourse.bass as bass
import concourse.mybir as mybir
from concourse.alu_op_type import AluOpType as OP
from concourse.bass_utils import run_bass_kernel_spmd

F32 = mybir.dt.float32
BF16 = mybir.dt.bfloat16
I8 = mybir.dt.int8
AF = mybir.ActivationFunctionType
AX = mybir.AxisListType

D = 2048
I_FFN = 8192
NFC = D // 128
NIC = I_FFN // 128
EPS_LN = 1e-5
EPS_Q = 1e-5
EPS_N = 1e-6
C1 = float(1.0 / np.sqrt(2048.0))
C2 = float(1.0 / np.sqrt(8192.0))

ENGS = ("sync", "scalar", "vector", "tensor", "gpsimd")
SEMS = ("s_pe", "s_dve", "s_act", "s_pool")
ENG_SEM = {"tensor": "s_pe", "vector": "s_dve", "scalar": "s_act",
           "gpsimd": "s_pool"}


class TK:

    def __init__(self, nc, sems, es):
        self.nc = nc
        self.sems = dict(sems)
        self.es = es
        self.cnt = {s: 0 for s in self.sems}
        self.ops = {e: [] for e in ENGS}
        self.floor = {e: {} for e in ENGS}
        self.writer = {}
        self.readers = {}
        self.nwaits = 0
        self.ninst = 0

    def _dma_sem(self, key, eng):
        cls = "s" if eng == "gpsimd" else "h"
        name = f"dq{cls}:{key}"
        if name not in self.sems:
            self.sems[name] = self.es.enter_context(
                self.nc.semaphore(name.replace(":", "_")))
            self.cnt[name] = 0
        return name

    def emit(self, eng, fn, reads=(), writes=(), dma=False, inc=True,
             dma_group=None):
        if dma:
            mystream = self._dma_sem(dma_group if dma_group else writes[0], eng)
        else:
            mystream = ENG_SEM[eng]
        deps = {}
        for kk in reads:
            w = self.writer.get(kk)
            if w and deps.get(w[0], 0) < w[1]:
                deps[w[0]] = w[1]
        for kk in writes:
            w = self.writer.get(kk)
            if w and deps.get(w[0], 0) < w[1]:
                deps[w[0]] = w[1]
            for s, v in self.readers.get(kk, {}).items():
                if deps.get(s, 0) < v:
                    deps[s] = v
        for s in sorted(deps):
            v = deps[s]
            if eng == "tensor" and not dma and s == "s_pe":
                continue
            if s.startswith("dq"):
                v = self.cnt[s]
            if v <= 0 or self.floor[eng].get(s, 0) >= v:
                continue
            self.floor[eng][s] = v
            sem = self.sems[s]
            self.nwaits += 1
            self.ops[eng].append(lambda e, ss=sem, vv=v: e.wait_ge(ss, vv))
        incval = 16 if dma else 1
        self.ninst += 1
        if inc:
            self.cnt[mystream] += incval
            sem = self.sems[mystream]
            self.ops[eng].append(
                lambda e, f=fn, ss=sem, ii=incval: f(e).then_inc(ss, ii))
            val = self.cnt[mystream]
        else:
            self.ops[eng].append(lambda e, f=fn: f(e))
            val = self.cnt[mystream] + incval
        for kk in writes:
            self.writer[kk] = (mystream, val)
            self.readers[kk] = {}
        for kk in reads:
            d = self.readers.setdefault(kk, {})
            if d.get(mystream, 0) < val:
                d[mystream] = val

    def raw(self, eng, fn):
        self.ops[eng].append(fn)


def _hadamard(n):
    h = np.ones((1, 1), dtype=np.float64)
    while h.shape[0] < n:
        h = np.block([[h, h], [h, -h]])
    return h


def build(own_tiles=8, all_tiles=16, phases=99):
    import os as _os
    P1T = int(_os.environ.get("K_P1T", "999"))
    W1N = int(_os.environ.get("K_W1N", "99999"))
    FIN = int(_os.environ.get("K_FIN", "1"))
    TRS = int(_os.environ.get("K_TRS", "1"))
    OT, AT = own_tiles, all_tiles
    assert OT % 2 == 0 and AT == 2 * OT
    T_OWN = 128 * OT
    WW = T_OWN // 2
    TPW = OT // 2
    NHW = max(1, T_OWN // 512)
    HWW = T_OWN // NHW
    NW = {"g": float(I_FFN * D), "u": float(I_FFN * D), "d": float(D * I_FFN)}

    nc = bass.Bass("TRN2", target_bir_lowering=False, debug=False)
    dram = lambda n, s, dt, kk: nc.dram_tensor(n, s, dt, kind=kk).ap()
    xb = dram("xb", [128 * AT, D], F32, "ExternalInput")
    wq_h = dram("wq", [D, D], F32, "ExternalInput")
    wk_h = dram("wk", [D, D], F32, "ExternalInput")
    wv_h = dram("wv", [D, D], F32, "ExternalInput")
    wo_h = dram("wo", [D, D], F32, "ExternalInput")
    bo_h = dram("bo", [1, D], F32, "ExternalInput")
    wg_h = dram("wg", [I_FFN, D], F32, "ExternalInput")
    wu_h = dram("wu", [I_FFN, D], F32, "ExternalInput")
    wd_h = dram("wd", [D, I_FFN], F32, "ExternalInput")
    identb_h = dram("identb", [128, 128], BF16, "ExternalInput")
    identf_h = dram("identf", [128, 128], F32, "ExternalInput")
    h128_h = dram("h128", [128, 128], BF16, "ExternalInput")
    onescb_h = dram("ones_cb", [128, 1], BF16, "ExternalInput")
    onesrb_h = dram("ones_rb", [1, 128], BF16, "ExternalInput")
    onesrf_h = dram("ones_rf", [1, 128], F32, "ExternalInput")
    onescf_h = dram("ones_cf", [128, 1], F32, "ExternalInput")
    selh_h = dram("selh", [32, 16 * 128], BF16, "ExternalInput")
    out_h = dram("out", [T_OWN, D], F32, "ExternalOutput")
    tern = {"g": dram("tern_g", [I_FFN, D], BF16, "Internal"),
            "u": dram("tern_u", [I_FFN, D], BF16, "Internal"),
            "d": dram("tern_d", [D, I_FFN], BF16, "Internal")}
    x1sp = dram("x1sp", [T_OWN, D], F32, "Internal")
    dsp = dram("dsp", [D, T_OWN // 2], BF16, "Internal")

    es = ExitStack()
    sb = lambda n, s, dt: es.enter_context(nc.sbuf_tensor(n, s, dt))
    ps = lambda n, s, dt: es.enter_context(nc.psum_tensor(n, s, dt))

    BIGW = max(D, 2 * T_OWN)
    bbA = [sb(f"bbA{i}", [128, BIGW], BF16) for i in range(8)]
    bbB = [sb(f"bbB{i}", [128, BIGW], BF16) for i in range(8)]
    bbC = [sb(f"bbC{i}", [128, BIGW], BF16) for i in range(8)]
    btmp = sb("btmp", [128, T_OWN], BF16)
    cabs = sb("cabs", [128, 2 * WW], BF16)

    def yfm(c):
        return bbA[c // 2][:, (c % 2) * T_OWN:(c % 2) * T_OWN + T_OWN]

    def yfmk(c):
        return f"bbA{c // 2}"

    def qfm(c):
        return bbB[c // 2][:, (c % 2) * T_OWN:(c % 2) * T_OWN + T_OWN]

    def qfmk(c):
        return f"bbB{c // 2}"

    f1v, f1k = qfm, qfmk


    def attnf(t):
        return bbC[t][:, 0:D]

    def attnfk(t):
        return f"bbC{t}"

    def hv(c):
        if c < 32:
            return bbA[c // 4][:, (c % 4) * WW:(c % 4) * WW + WW]
        c -= 32
        return bbC[c // 4][:, (c % 4) * WW:(c % 4) * WW + WW]

    def hk(c):
        return f"bbA{c // 4}" if c < 32 else f"bbC{(c - 32) // 4}"

    c_identb = sb("c_identb", [128, 128], BF16)
    c_identf = sb("c_identf", [128, 128], F32)
    c_h128 = sb("c_h128", [128, 128], BF16)
    c_onescb = sb("c_onescb", [128, 1], BF16)
    c_onesrb = sb("c_onesrb", [1, 128], BF16)
    c_onesrf = sb("c_onesrf", [1, 128], F32)
    c_onescf = sb("c_onescf", [128, 1], F32)
    c_sel = sb("c_sel", [32, 16 * 128], BF16)
    bo_bf = sb("bo_bf", [1, D], BF16)
    xst = [sb(f"xst{i}", [128, D], F32) for i in range(2)]
    wqst = [sb(f"wqst{i}", [128, 2048], F32) for i in range(2)]
    outbuf = wqst
    ybuf = sb("ybuf", [128, D], F32)
    gf4 = sb("gf4", [128, max(2048, 4 * WW)], BF16)
    uf = sb("uf", [128, max(WW, HWW)], F32)
    rbsb = sb("rbsb", [128, HWW], F32)
    scB = sb("scB", [128, WW], F32)
    egB = sb("egB", [128, WW], F32)
    euB = sb("euB", [128, WW], F32)
    edB = egB
    aq_a = uf
    wstq = [sb(f"wstq{c}", [128, 256], BF16) for c in range(NFC)]
    terns = gf4
    tquant = sb("tquant", [128, 2048], BF16)
    ytm = sb("ytm", [128, D], BF16)
    kbuf = [sb(f"kbuf{i}", [128, 256], BF16) for i in range(2)]
    vbuf = [sb(f"vbuf{i}", [128, 256], BF16) for i in range(2)]
    kvbd = [sb(f"kvbd{j}", [128, 128], BF16) for j in range(16)]
    ksm = [sb(f"ksm{j}", [128, 32], BF16) for j in range(16)]
    rinvb = sb("rinvb", [32, T_OWN], BF16)
    wT = [sb(f"wT{c}", [128, 512], BF16) for c in range(NFC)]
    wTd = [sb(f"wTd{i}", [128, 512], BF16) for i in range(2)]
    aq_i8 = sb("aq_i8", [128, WW], I8)
    aq_d = rbsb
    srow = rbsb[0:1, 0:WW]
    aq_s1 = ytm[:, 0:WW]
    aq_s2 = ytm[:, WW:2 * WW]
    aq_u = ytm[:, 2 * WW:3 * WW]
    tot_w = {w: sb(f"tot_{w}", [128, 1], F32) for w in ("g", "u", "d")}
    s_sc = sb("s_sc", [1, 4], F32)
    thr_col = {w: sb(f"thr_{w}", [128, 1], F32) for w in ("g", "u", "d")}
    nthr_col = {w: sb(f"nthr_{w}", [128, 1], F32) for w in ("g", "u", "d")}
    swcol = {w: sb(f"swcol_{w}", [128, 1], F32) for w in ("g", "u", "d")}
    lncols = sb("lncols", [128, 12 + TPW], F32)
    accp = lncols[:, 0:1]
    ln_sum = lncols[:, 1:2]
    ln_sq = lncols[:, 2:3]
    ln_mu = lncols[:, 3:4]
    ln_t = lncols[:, 4:5]
    ln_var = lncols[:, 5:6]
    ln_rs = lncols[:, 6:7]
    ln_nb = lncols[:, 7:8]
    mx_c = lncols[:, 8:9]
    sA_c = lncols[:, 9:10]
    e_c = lncols[:, 10:11]
    mt_c = [lncols[:, 12 + t:13 + t] for t in range(TPW)]


    pb = [ps(f"pb{i}", [128, 512], F32) for i in range(6)]
    ptrbank = [ps("ptrA", [128, 1024], BF16), ps("ptrB", [128, 1024], BF16)]

    sems = {s: es.enter_context(nc.semaphore(s)) for s in SEMS}
    tk = TK(nc, sems, es)

    def dve(fn, reads, writes, inc=True):
        tk.emit("vector", fn, reads, writes, inc=inc)

    def act(fn, reads, writes):
        tk.emit("scalar", fn, reads, writes)

    def pe(fn, reads, writes, inc=True):
        tk.emit("tensor", fn, reads, writes, inc=inc)

    def dma(q, fn, reads, writes, dma_group=None):
        tk.emit(q, fn, reads, writes, dma=True, dma_group=dma_group)

    TAP = _os.environ.get("K_TAP", "")

    def tap(ap, key, width):
        tk.emit("gpsimd", lambda e, a=ap, w=width: e.dma_start(
            out=out_h[0:128, 0:w], in_=a), [key], ["out_h"], dma=True,
            dma_group="out_tap")

    def k(t):
        return t.name if hasattr(t, "name") else t.tensor.name

    for dst, srch in [(c_identb, identb_h), (c_identf, identf_h),
                      (c_h128, h128_h), (c_onescb, onescb_h),
                      (c_onesrb, onesrb_h), (c_onesrf, onesrf_h),
                      (c_onescf, onescf_h), (c_sel, selh_h),
                      (bo_bf, bo_h)]:
        if dst is bo_bf:
            tk.emit("gpsimd", lambda e, d=dst, s=srch: e.dma_start(
                out=d[:], in_=s[:]), [], [k(dst)], dma=True,
                dma_group="bo_ld")
        else:
            tk.emit("sync", lambda e, d=dst, s=srch: e.dma_start(
                out=d[:], in_=s[:]), [], [k(dst)], dma=True,
                dma_group="consts")

    def ln(x_ap, xkey, out_ap, outkey):
        dve(lambda e: e.reduce_sum(out=ln_sum[:], in_=x_ap, axis=AX.X),
            [xkey], [k(ln_sum)])
        act(lambda e: e.activation(out=ybuf[:], in_=x_ap,
                                   func=AF.Square, accum_out=ln_sq[:]),
            [xkey], [k(ybuf), k(ln_sq)])
        dve(lambda e: e.tensor_scalar(out=ln_mu[:], in0=ln_sum[:],
                                      scalar1=1.0 / D, scalar2=None,
                                      op0=OP.mult), [k(ln_sum)], [k(ln_mu)])
        dve(lambda e: e.tensor_scalar(out=ln_t[:], in0=ln_sq[:],
                                      scalar1=1.0 / D, scalar2=None,
                                      op0=OP.mult), [k(ln_sq)], [k(ln_t)])
        dve(lambda e: e.tensor_tensor(out=ln_var[:], in0=ln_mu[:],
                                      in1=ln_mu[:], op=OP.mult),
            [k(ln_mu)], [k(ln_var)])
        dve(lambda e: e.tensor_tensor(out=ln_var[:], in0=ln_t[:],
                                      in1=ln_var[:], op=OP.subtract),
            [k(ln_t), k(ln_var)], [k(ln_var)])
        act(lambda e: e.activation(out=ln_rs[:], in_=ln_var[:], func=AF.Ln,
                                   bias=np.float32(EPS_LN), scale=1.0),
            [k(ln_var)], [k(ln_rs)])
        act(lambda e: e.activation(out=ln_rs[:], in_=ln_rs[:], func=AF.Exp,
                                   bias=0.0, scale=np.float32(-0.5)),
            [k(ln_rs)], [k(ln_rs)])
        dve(lambda e: e.tensor_tensor(out=ln_t[:], in0=ln_rs[:], in1=ln_rs[:],
                                      op=OP.mult),
            [k(ln_rs)], [k(ln_t)])
        dve(lambda e: e.tensor_scalar(out=ln_nb[:], in0=ln_var[:],
                                      scalar1=float(EPS_LN), scalar2=-0.5,
                                      op0=OP.add, op1=OP.mult),
            [k(ln_var)], [k(ln_nb)])
        dve(lambda e: e.tensor_tensor(out=ln_t[:], in0=ln_t[:], in1=ln_nb[:],
                                      op=OP.mult),
            [k(ln_t), k(ln_nb)], [k(ln_t)])
        dve(lambda e: e.tensor_scalar(out=ln_t[:], in0=ln_t[:], scalar1=1.5,
                                      scalar2=None, op0=OP.add),
            [k(ln_t)], [k(ln_t)])
        dve(lambda e: e.tensor_tensor(out=ln_rs[:], in0=ln_rs[:], in1=ln_t[:],
                                      op=OP.mult),
            [k(ln_rs), k(ln_t)], [k(ln_rs)])
        dve(lambda e: e.tensor_tensor(out=ln_nb[:], in0=ln_mu[:],
                                      in1=ln_rs[:], op=OP.mult),
            [k(ln_mu), k(ln_rs)], [k(ln_nb)])
        dve(lambda e: e.tensor_scalar(out=ln_nb[:], in0=ln_nb[:], scalar1=-1.0,
                                      scalar2=None, op0=OP.mult),
            [k(ln_nb)], [k(ln_nb)])
        act(lambda e: e.activation(out=out_ap, in_=x_ap, func=AF.Identity,
                                   bias=ln_nb[:], scale=ln_rs[:]),
            [xkey, k(ln_nb), k(ln_rs)], [outkey])

    w_tiles = []
    for wid, wap in (("g", wg_h), ("u", wu_h)):
        for oc in range(I_FFN // 128):
            w_tiles.append((wid, wap, oc * 128, 0))
    for qrt in range(4):
        for oc in range(D // 128):
            w_tiles.append(("d", wd_h, oc * 128, qrt * 2048))

    for wid in ("g", "u", "d"):
        dve(lambda e, w=wid: e.memset(tot_w[w][:], 0.0), [], [k(tot_w[wid])])

    w1_i = [0]

    def w1_step():
        i = w1_i[0]
        if i >= len(w_tiles) or i >= W1N:
            return False
        w1_i[0] += 1
        wid, wap, r0, c0 = w_tiles[i]
        stg = wqst[i % 2]
        dma("scalar", lambda e, a=wap, r=r0, c=c0, d=stg: e.dma_start(
            out=d[:], in_=a[r:r + 128, c:c + 2048]), [], [k(stg)])
        if (i % 3) != 2:
            act(lambda e, d=stg: e.activation(out=ybuf[:], in_=d[:, 0:D],
                                              func=AF.Abs, accum_out=accp[:]),
                [k(stg)], [k(ybuf), k(accp)])
            dve(lambda e, w=wid: e.tensor_tensor(out=tot_w[w][:],
                                                 in0=tot_w[w][:], in1=accp[:],
                                                 op=OP.add),
                [k(tot_w[wid]), k(accp)], [k(tot_w[wid])])
        else:
            dve(lambda e, d=stg: e.tensor_scalar(out=ybuf[:], in0=d[:, 0:D],
                                                 scalar1=0.0, scalar2=None,
                                                 op0=OP.max, op1=OP.add,
                                                 accum_out=accp[:]),
                [k(stg)], [k(ybuf), k(accp)])
            dve(lambda e, w=wid: e.tensor_tensor(out=tot_w[w][:],
                                                 in0=tot_w[w][:], in1=accp[:],
                                                 op=OP.add),
                [k(tot_w[wid]), k(accp)], [k(tot_w[wid])])
            dve(lambda e, d=stg: e.tensor_scalar(out=ybuf[:], in0=d[:, 0:D],
                                                 scalar1=0.0, scalar2=None,
                                                 op0=OP.min, op1=OP.add,
                                                 accum_out=accp[:]),
                [k(stg)], [k(ybuf), k(accp)])
            dve(lambda e, w=wid: e.tensor_tensor(out=tot_w[w][:],
                                                 in0=tot_w[w][:], in1=accp[:],
                                                 op=OP.subtract),
                [k(tot_w[wid]), k(accp)], [k(tot_w[wid])])
        return True

    def w1_finalize():
        for j, wid in enumerate(("g", "u", "d")):
            pe(lambda e, w=wid: e.matmul(pb[5][0:1, 0:1], c_onescf[:],
                                         tot_w[w][:], start=True, stop=True),
               [k(c_onescf), k(tot_w[wid])], ["pb5"])
            dve(lambda e, w=wid, jj=j: e.tensor_scalar(
                out=s_sc[0:1, jj:jj + 1], in0=pb[5][0:1, 0:1],
                scalar1=1.0 / NW[w], scalar2=EPS_Q, op0=OP.mult, op1=OP.max),
                ["pb5"], [k(s_sc)])
            dve(lambda e, jj=j: e.tensor_scalar(
                out=s_sc[0:1, 3:4], in0=s_sc[0:1, jj:jj + 1], scalar1=0.5,
                scalar2=None, op0=OP.mult), [k(s_sc)], [k(s_sc)])
            pe(lambda e: e.matmul(pb[5][:, 0:1], c_onesrf[:], s_sc[0:1, 3:4],
                                  start=True, stop=True),
               [k(c_onesrf), k(s_sc)], ["pb5"])
            dve(lambda e, w=wid: e.tensor_copy(out=thr_col[w][:],
                                               in_=pb[5][:, 0:1]),
                ["pb5"], [k(thr_col[wid])])
            dve(lambda e, w=wid: e.tensor_scalar(out=nthr_col[w][:],
                                                 in0=thr_col[w][:],
                                                 scalar1=-1.0, scalar2=None,
                                                 op0=OP.mult),
                [k(thr_col[wid])], [k(nthr_col[wid])])
            pe(lambda e, jj=j: e.matmul(pb[5][:, 0:1], c_onesrf[:],
                                        s_sc[0:1, jj:jj + 1], start=True,
                                        stop=True),
               [k(c_onesrf), k(s_sc)], ["pb5"])
            dve(lambda e, w=wid: e.tensor_copy(out=swcol[w][:],
                                               in_=pb[5][:, 0:1]),
                ["pb5"], [k(swcol[wid])])

    w2_i = [0]

    def w2_step():
        i = w2_i[0]
        if i >= len(w_tiles):
            return False
        w2_i[0] += 1
        wid, wap, r0, c0 = w_tiles[i]
        stg = wqst[i % 2]
        dma("scalar", lambda e, a=wap, r=r0, c=c0, d=stg: e.dma_start(
            out=d[:], in_=a[r:r + 128, c:c + 2048]), [], [k(stg)])
        if (i % 2) == 0:
            act(lambda e, w=wid, d=stg: e.activation(
                out=tquant[:], in_=d[:], func=AF.Sign, bias=nthr_col[w][:],
                scale=1.0), [k(stg), k(nthr_col[wid])], [k(tquant)])
            act(lambda e, w=wid, d=stg: e.activation(
                out=terns[:, 0:2048], in_=d[:], func=AF.Sign, bias=thr_col[w][:],
                scale=1.0), [k(stg), k(thr_col[wid])], [k(terns)])
        else:
            dve(lambda e, w=wid, d=stg: e.tensor_scalar(
                out=tquant[:], in0=d[:], scalar1=thr_col[w][:], scalar2=2.0,
                op0=OP.is_ge, op1=OP.mult),
                [k(stg), k(thr_col[wid])], [k(tquant)])
            dve(lambda e, w=wid, d=stg: e.tensor_scalar(
                out=terns[:, 0:2048], in0=d[:], scalar1=nthr_col[w][:],
                scalar2=-2.0, op0=OP.is_le, op1=OP.mult),
                [k(stg), k(nthr_col[wid])], [k(terns)])
        dve(lambda e: e.tensor_tensor(out=terns[:, 0:2048], in0=tquant[:],
                                      in1=terns[:, 0:2048], op=OP.add),
            [k(tquant), k(terns)], [k(terns)])
        dma("gpsimd", lambda e, w=wid, r=r0, c=c0: e.dma_start(
            out=tern[w][r:r + 128, c:c + 2048], in_=terns[:, 0:2048]),
            [k(terns)], [f"tern_{wid}"])
        return True

    def w1steps(n):
        for _ in range(n):
            if not w1_step():
                return

    def w2steps(n):
        for _ in range(n):
            if not w2_step():
                return

    def pe_tr128(src_ap, src_key, dst_ap, dst_key, i):
        bank = i % 2
        s = (i // 2) % 8
        pt = ptrbank[bank]
        pkey = "ptrA" if bank == 0 else "ptrB"
        pe(lambda e, ss=s, p=pt: e.transpose(p[:, ss * 128:(ss + 1) * 128],
                                             src_ap, c_identb[:]),
           [src_key, k(c_identb), pkey], [pkey])
        dve(lambda e, ss=s, p=pt: e.tensor_copy(
            out=dst_ap, in_=p[:, ss * 128:(ss + 1) * 128]),
            [pkey], [dst_key, pkey])

    kv_total = 2 * 4 * (AT // 2) * 2
    ks_total = 2 * (D // 256) * (AT // 2) * 2
    kv_cnt = {(0, 0): 0, (0, 1): 0, (1, 0): 0, (1, 1): 0}
    ks_cnt = {0: 0, 1: 0}

    for half in range(2):
        for tt in range(AT // 2):
            if half * (AT // 2) + tt >= P1T:
                break
            t = half * (AT // 2) + tt
            xt = xst[tt % 2]
            dma("sync", lambda e, T=t, d=xt: e.dma_start(
                out=d[:], in_=xb[T * 128:(T + 1) * 128, :]), [], [k(xt)])
            ln(xt[:], k(xt), ytm[:], k(ytm))
            for c in range(NFC if TRS else 0):
                pe_tr128(ytm[:, c * 128:(c + 1) * 128], k(ytm),
                         yfm(c)[:, tt * 128:(tt + 1) * 128], yfmk(c), c)
            w1steps(6)
        if half == 0:
            if TAP == "yfm0":
                tap(yfm(0)[:, 0:T_OWN], yfmk(0), T_OWN)
            while w1_step():
                pass
            if FIN:
                w1_finalize()
        for og in range(D // 256 if phases >= 2 else 0):
            for c in range(NFC):
                dma("gpsimd", lambda e, cc=c, o=og: e.dma_start(
                    out=wstq[cc][:], in_=wk_h[cc * 128:(cc + 1) * 128,
                                             o * 256:(o + 1) * 256]),
                    [], [k(wstq[c])])
                dma("gpsimd", lambda e, cc=c, o=og: e.dma_start(
                    out=wT[cc][:, 0:256], in_=wv_h[cc * 128:(cc + 1) * 128,
                                                   o * 256:(o + 1) * 256]),
                    [], [k(wT[c])])
            for tt in range(AT // 2):
                kb = kbuf[tt % 2]
                vb = vbuf[tt % 2]
                for c in range(NFC):
                    pe(lambda e, cc=c, T=tt: e.matmul(
                        pb[3][:, 0:256], yfm(cc)[:, T * 128:(T + 1) * 128],
                        wstq[cc][:], start=(cc == 0), stop=(cc == NFC - 1)),
                       [yfmk(c), k(wstq[c]), "pb3"], ["pb3"],
                       inc=(c == NFC - 1))
                act(lambda e, K=kb: e.activation(out=K[:], in_=pb[3][:, 0:256],
                                                 func=AF.Relu),
                    ["pb3"], [k(kb)])
                for c in range(NFC):
                    pe(lambda e, cc=c, T=tt: e.matmul(
                        pb[4][:, 0:256], yfm(cc)[:, T * 128:(T + 1) * 128],
                        wT[cc][:, 0:256], start=(cc == 0),
                        stop=(cc == NFC - 1)),
                       [yfmk(c), k(wT[c]), "pb4"], ["pb4"],
                       inc=(c == NFC - 1))
                dve(lambda e, V=vb: e.tensor_copy(out=V[:],
                                                  in_=pb[4][:, 0:256]),
                    ["pb4"], [k(vb)])
                for j in range(2):
                    jg = og * 2 + j
                    bank = jg // 8
                    col = (jg % 8) * 64
                    for par in range(2):
                        kv_cnt[(bank, par)] += 1
                        a0 = kv_cnt[(bank, par)] == 1
                        z0 = kv_cnt[(bank, par)] == kv_total
                        c0 = j * 128 + par * 64
                        if par == 0:
                            pe(lambda e, b=bank, cl=col, K=kb, V=vb, a=a0,
                               z=z0, cc=c0: e.matmul(
                                   pb[b][0:64, cl:cl + 64], K[:, cc:cc + 64],
                                   V[:, cc:cc + 64], start=a, stop=z,
                                   skip_group_check=True),
                               [k(kb), k(vb), f"pb{bank}"], [f"pb{bank}"],
                               inc=z0)
                        else:
                            pe(lambda e, b=bank, cl=col, K=kb, V=vb, a=a0,
                               z=z0, cc=c0: e.matmul(
                                   pb[b][64:128, cl:cl + 64], K[:, cc:cc + 64],
                                   V[:, cc:cc + 64], start=a, stop=z,
                                   tile_position=(0, 64),
                                   skip_group_check=True),
                               [k(kb), k(vb), f"pb{bank}"], [f"pb{bank}"],
                               inc=z0)
                    for par in range(2):
                        ks_cnt[par] += 1
                        a0 = ks_cnt[par] == 1
                        z0 = ks_cnt[par] == ks_total
                        c0 = j * 128 + par * 64
                        if par == 0:
                            pe(lambda e, g=jg, K=kb, a=a0, z=z0, cc=c0:
                               e.matmul(pb[2][0:64, g:g + 1],
                                        K[:, cc:cc + 64], c_onescb[:],
                                        start=a, stop=z,
                                        skip_group_check=True),
                               [k(kb), k(c_onescb), "pb2"], ["pb2"], inc=z0)
                        else:
                            pe(lambda e, g=jg, K=kb, a=a0, z=z0, cc=c0:
                               e.matmul(pb[2][64:128, g:g + 1],
                                        K[:, cc:cc + 64], c_onescb[:],
                                        start=a, stop=z,
                                        tile_position=(0, 64),
                                        skip_group_check=True),
                               [k(kb), k(c_onescb), "pb2"], ["pb2"], inc=z0)
                w2steps(1)
        if half == 0 and phases >= 3:
            for icg in range(D // 256):
                for c in range(NFC):
                    dma("gpsimd", lambda e, cc=c, o=icg: e.dma_start(
                        out=wstq[cc][:], in_=wq_h[cc * 128:(cc + 1) * 128,
                                                 o * 256:(o + 1) * 256]),
                        [], [k(wstq[c])])
                for ics in range(2):
                    ic = icg * 2 + ics
                    for hw in range(NHW):
                        b = 3 + (hw % 2)
                        for c in range(NFC):
                            pe(lambda e, cc=c, s=ics, H=hw, bb=b: e.matmul(
                                pb[bb][:, 0:HWW],
                                wstq[cc][:, s * 128:(s + 1) * 128],
                                yfm(cc)[:, H * HWW:(H + 1) * HWW],
                                start=(cc == 0), stop=(cc == NFC - 1)),
                               [k(wstq[c]), yfmk(c), f"pb{b}"], [f"pb{b}"],
                               inc=(c == NFC - 1))
                        act(lambda e, i=ic, H=hw, bb=b: e.activation(
                            out=qfm(i)[:, H * HWW:(H + 1) * HWW],
                            in_=pb[bb][:, 0:HWW], func=AF.Relu),
                            [f"pb{b}"], [qfmk(ic)])
                    w2steps(2)

    if TAP == "qfm0":
        tap(qfm(0)[:, 0:T_OWN], qfmk(0), T_OWN)
    while phases >= 3 and w2_step():
        pass

    for jg in range(16 if phases >= 4 else 0):
        bank = jg // 8
        col = (jg % 8) * 64
        dve(lambda e, j=jg: e.memset(kvbd[j][:], 0.0), [], [k(kvbd[jg])])
        dve(lambda e, j=jg, b=bank, cl=col: e.tensor_copy(
            out=kvbd[j][0:64, 0:64], in_=pb[b][0:64, cl:cl + 64]),
            [f"pb{bank}"], [k(kvbd[jg])])
        dve(lambda e, j=jg, b=bank, cl=col: e.tensor_copy(
            out=kvbd[j][64:128, 64:128], in_=pb[b][64:128, cl:cl + 64]),
            [f"pb{bank}"], [k(kvbd[jg])])
        dve(lambda e, j=jg: e.memset(ksm[j][:], 0.0), [], [k(ksm[jg])])
        dve(lambda e, j=jg: e.tensor_copy(
            out=ksm[j][0:64, 2 * j:2 * j + 1], in_=pb[2][0:64, j:j + 1]),
            ["pb2"], [k(ksm[jg])])
        dve(lambda e, j=jg: e.tensor_copy(
            out=ksm[j][64:128, 2 * j + 1:2 * j + 2],
            in_=pb[2][64:128, j:j + 1]),
            ["pb2"], [k(ksm[jg])])
    for hw in range(NHW if phases >= 4 else 0):
        for jg in range(16):
            pe(lambda e, j=jg, H=hw: e.matmul(
                pb[5][0:32, 0:HWW], ksm[j][:],
                qfm(j)[:, H * HWW:(H + 1) * HWW],
                start=(j == 0), stop=(j == 15)),
               [k(ksm[jg]), qfmk(jg), "pb5"], ["pb5"], inc=(jg == 15))
        dve(lambda e: e.tensor_scalar(out=uf[0:32, 0:HWW],
                                      in0=pb[5][0:32, 0:HWW],
                                      scalar1=EPS_N, scalar2=None, op0=OP.max),
            ["pb5"], [k(uf)])
        dve(lambda e: e.reciprocal(out=uf[0:32, 0:HWW], in_=uf[0:32, 0:HWW]),
            [k(uf)], [k(uf)])
        dve(lambda e, H=hw: e.tensor_copy(
            out=rinvb[:, H * HWW:(H + 1) * HWW], in_=uf[0:32, 0:HWW]),
            [k(uf)], [k(rinvb)])
    for jg in range(16 if phases >= 4 else 0):
        for hw in range(NHW):
            pe(lambda e, j=jg, H=hw: e.matmul(
                pb[3][:, 0:HWW], kvbd[j][:],
                qfm(j)[:, H * HWW:(H + 1) * HWW], start=True, stop=True),
               [k(kvbd[jg]), qfmk(jg), "pb3"], ["pb3"])
            pe(lambda e, j=jg, H=hw: e.matmul(
                pb[5][:, 0:HWW], c_sel[:, j * 128:(j + 1) * 128],
                rinvb[:, H * HWW:(H + 1) * HWW], start=True, stop=True),
               [k(c_sel), k(rinvb), "pb5"], ["pb5"])
            dve(lambda e: e.tensor_copy(out=rbsb[:], in_=pb[5][:, 0:HWW]),
                ["pb5"], [k(rbsb)])
            dve(lambda e, j=jg, H=hw: e.tensor_tensor(
                out=qfm(j)[:, H * HWW:(H + 1) * HWW], in0=pb[3][:, 0:HWW],
                in1=rbsb[:], op=OP.mult),
                ["pb3", k(rbsb)], [qfmk(jg)])

    if TAP == "kvbd":
        for j in range(4):
            tap(kvbd[j][:, 0:128], k(kvbd[j]), 128)
    if TAP == "qkvn":
        tap(qfm(0)[:, 0:T_OWN], qfmk(0), T_OWN)
    if TAP == "rinv":
        tap(rinvb[0:32, 0:T_OWN], k(rinvb), T_OWN)
    for fg in range(D // 256 if phases >= 5 else 0):
        for c in range(NFC):
            dma("gpsimd", lambda e, cc=c, o=fg: e.dma_start(
                out=wstq[cc][:], in_=wo_h[cc * 128:(cc + 1) * 128,
                                          o * 256:(o + 1) * 256]),
                [], [k(wstq[c])])
        for t in range(OT):
            b = 3 + (t % 2)
            for c in range(NFC):
                pe(lambda e, cc=c, T=t, bb=b: e.matmul(
                    pb[bb][:, 0:256], qfm(cc)[:, T * 128:(T + 1) * 128],
                    wstq[cc][:], start=(cc == 0), stop=False),
                   [qfmk(c), k(wstq[c]), f"pb{b}"], [f"pb{b}"], inc=False)
            pe(lambda e, o=fg, bb=b: e.matmul(
                pb[bb][:, 0:256], c_onesrb[:],
                bo_bf[:, o * 256:(o + 1) * 256], start=False, stop=True),
               [k(c_onesrb), k(bo_bf), f"pb{b}"], [f"pb{b}"])
            dve(lambda e, T=t, o=fg, bb=b: e.tensor_copy(
                out=attnf(T)[:, o * 256:(o + 1) * 256], in_=pb[bb][:, 0:256]),
                [f"pb{b}"], [attnfk(t)])

    if TAP == "attnf":
        tap(attnf(0)[:, 0:D], attnfk(0), D)
    for t in range(OT if phases >= 5 else 0):
        xt = xst[t % 2]
        dma("sync", lambda e, T=t, d=xt: e.dma_start(
            out=d[:], in_=xb[T * 128:(T + 1) * 128, :]), [], [k(xt)])
        ln(attnf(t)[:], attnfk(t), ybuf[:], k(ybuf))
        dve(lambda e, d=xt: e.tensor_tensor(out=d[:], in0=d[:], in1=ybuf[:],
                                            op=OP.add),
            [k(xt), k(ybuf)], [k(xt)])
        dma("sync", lambda e, T=t, d=xt: e.dma_start(
            out=x1sp[T * 128:(T + 1) * 128, :], in_=d[:]), [k(xt)], ["x1sp"])
        ln(xt[:], k(xt), ytm[:], k(ytm))
        for c in range(NFC):
            pe_tr128(ytm[:, c * 128:(c + 1) * 128], k(ytm),
                     f1v(c)[:, t * 128:(t + 1) * 128], f1k(c), c)

    if TAP == "f1tm":
        tap(f1v(0)[:, 0:T_OWN], f1k(0), T_OWN)
    for c in range(NFC if phases >= 6 else 0):
        for hw in range(NHW):
            b = 3 + (hw % 2)
            pe(lambda e, cc=c, H=hw, bb=b: e.matmul(
                pb[bb][:, 0:HWW], c_h128[:],
                f1v(cc)[:, H * HWW:(H + 1) * HWW], start=True, stop=True),
               [k(c_h128), f1k(c), f"pb{b}"], [f"pb{b}"])
            dve(lambda e, cc=c, H=hw, bb=b: e.tensor_copy(
                out=f1v(cc)[:, H * HWW:(H + 1) * HWW], in_=pb[bb][:, 0:HWW]),
                [f"pb{b}"], [f1k(c)])

    BFLY = int(_os.environ.get("K_BFLY", "1"))
    def bfly(get_v, nch, width):
        if not BFLY:
            return
        hstep = 1
        while hstep < nch:
            for cb in range(nch):
                if cb & hstep:
                    continue
                A, Akk = get_v(cb)
                B, Bkk = get_v(cb + hstep)
                dve(lambda e, a=A, bq=B, w=width: e.tensor_tensor(
                    out=btmp[:, 0:w], in0=a, in1=bq, op=OP.subtract),
                    [Akk, Bkk], [k(btmp)])
                dve(lambda e, a=A, bq=B: e.tensor_tensor(
                    out=a, in0=a, in1=bq, op=OP.add),
                    [Akk, Bkk], [Akk])
                dve(lambda e, bq=B, w=width: e.tensor_copy(
                    out=bq, in_=btmp[:, 0:w]),
                    [k(btmp)], [Bkk])
            hstep *= 2

    if TAP == "f1h":
        tap(f1v(0)[:, 0:T_OWN], f1k(0), T_OWN)
    if phases >= 6:
        bfly(lambda c: (f1v(c)[:, 0:T_OWN], f1k(c)), NFC, T_OWN)
    if TAP == "f1b":
        tap(f1v(0)[:, 0:T_OWN], f1k(0), T_OWN)

    def absfold(get_v, n, width):
        A, Akk = get_v(0)
        B, Bkk = get_v(1)
        dve(lambda e, a=A, bq=B, w=width: e.tensor_tensor(
            out=cabs[:, 0:w], in0=a, in1=bq, op=OP.max),
            [Akk, Bkk], ["cabs"])
        dve(lambda e, a=A, bq=B, w=width: e.tensor_tensor(
            out=cabs[:, WW:WW + w], in0=a, in1=bq, op=OP.min),
            [Akk, Bkk], ["cabs"])
        for i in range(2, n):
            Cv, Ckk = get_v(i)
            dve(lambda e, cv=Cv, w=width: e.tensor_tensor(
                out=cabs[:, 0:w], in0=cabs[:, 0:w], in1=cv, op=OP.max),
                [Ckk, "cabs"], ["cabs"])
            dve(lambda e, cv=Cv, w=width: e.tensor_tensor(
                out=cabs[:, WW:WW + w], in0=cabs[:, WW:WW + w], in1=cv,
                op=OP.min),
                [Ckk, "cabs"], ["cabs"])
        dve(lambda e, w=width: e.tensor_scalar(
            out=cabs[:, WW:WW + w], in0=cabs[:, WW:WW + w], scalar1=-1.0,
            scalar2=None, op0=OP.mult), ["cabs"], ["cabs"])
        dve(lambda e, w=width: e.tensor_tensor(
            out=cabs[:, 0:w], in0=cabs[:, 0:w], in1=cabs[:, WW:WW + w],
            op=OP.max), ["cabs"], ["cabs"])

    def bcast_row(dstap, dstkey, width):
        for sg in range((width + 511) // 512):
            wseg = min(512, width - sg * 512)
            pe(lambda e, s=sg, w=wseg: e.matmul(
                pb[5][:, 0:w], c_onesrf[:], srow[0:1, s * 512:s * 512 + w],
                start=True, stop=True),
               [k(c_onesrf), k(srow), "pb5"], ["pb5"])
            dve(lambda e, s=sg, w=wseg, dd=dstap: e.tensor_copy(
                out=dd[:, s * 512:s * 512 + w], in_=pb[5][:, 0:w]),
                ["pb5"], [dstkey])

    def scale_rows(cscale, width, ntiles, wlist):
        for ti in range(ntiles):
            bank = ti % 2
            s = (ti // 2) % 8
            pt = ptrbank[bank]
            pkey = "ptrA" if bank == 0 else "ptrB"
            pe(lambda e, ss=s, i=ti, p=pt: e.transpose(
                p[:, ss * 128:(ss + 1) * 128],
                cabs[:, i * 128:(i + 1) * 128], c_identb[:]),
               ["cabs", k(c_identb), pkey], [pkey])
            dve(lambda e, ss=s, p=pt: e.reduce_max(
                out=mx_c[:], in_=p[:, ss * 128:(ss + 1) * 128], axis=AX.X),
                [pkey], [k(mx_c), pkey])
            dve(lambda e, i=ti, cs=cscale: e.tensor_scalar(
                out=mt_c[i][:], in0=mx_c[:], scalar1=cs, scalar2=EPS_Q,
                op0=OP.mult, op1=OP.max), [k(mx_c)], [k(mt_c[ti])])
            dve(lambda e, i=ti: e.reciprocal(out=sA_c[:], in_=mt_c[i][:]),
                [k(mt_c[ti])], [k(sA_c)])
            dve(lambda e, cs=cscale: e.tensor_scalar(
                out=sA_c[:], in0=sA_c[:], scalar1=127.0 * cs, scalar2=None,
                op0=OP.mult), [k(sA_c)], [k(sA_c)])
            pe(lambda e: e.transpose(pb[5][0:1, 0:128], sA_c[:], c_identf[:]),
               [k(sA_c), k(c_identf), "pb5"], ["pb5"])
            dve(lambda e, i=ti: e.tensor_copy(
                out=srow[0:1, i * 128:(i + 1) * 128],
                in_=pb[5][0:1, 0:128]), ["pb5"], [k(srow)])
        bcast_row(scB, k(scB), width)
        for wid, dstap, dstkey in wlist:
            for ti in range(ntiles):
                dve(lambda e, i=ti, w=wid: e.tensor_tensor(
                    out=e_c[:], in0=mt_c[i][:], in1=swcol[w][:], op=OP.mult),
                    [k(mt_c[ti]), k(swcol[wid])], [k(e_c)])
                dve(lambda e: e.tensor_scalar(
                    out=e_c[:], in0=e_c[:], scalar1=1.0 / 508.0, scalar2=None,
                    op0=OP.mult), [k(e_c)], [k(e_c)])
                pe(lambda e: e.transpose(pb[5][0:1, 0:128], e_c[:],
                                         c_identf[:]),
                   [k(e_c), k(c_identf), "pb5"], ["pb5"])
                dve(lambda e, i=ti: e.tensor_copy(
                    out=srow[0:1, i * 128:(i + 1) * 128],
                    in_=pb[5][0:1, 0:128]), ["pb5"], [k(srow)])
            bcast_row(dstap, dstkey, width)

    NORND = int(_os.environ.get("K_NORND", "0"))

    def round_chunk(src_ap, src_key, width):
        if NORND:
            return
        dve(lambda e, s=src_ap, w=width: e.tensor_tensor(
            out=aq_a[:, 0:w], in0=s, in1=scB[:, 0:w], op=OP.mult),
            [src_key, k(scB)], [k(aq_a)])
        dve(lambda e, w=width: e.tensor_copy(out=aq_i8[:, 0:w],
                                             in_=aq_a[:, 0:w]),
            [k(aq_a)], [k(aq_i8)])
        dve(lambda e, w=width: e.tensor_tensor(out=aq_d[:, 0:w],
                                               in0=aq_a[:, 0:w],
                                               in1=aq_i8[:, 0:w],
                                               op=OP.subtract),
            [k(aq_a), k(aq_i8)], [k(aq_d)])
        act(lambda e, w=width: e.activation(out=aq_s1[:, 0:w],
                                            in_=aq_d[:, 0:w], func=AF.Sign,
                                            bias=np.float32(-0.5), scale=1.0),
            [k(aq_d)], [k(aq_s1)])
        act(lambda e, w=width: e.activation(out=aq_s2[:, 0:w],
                                            in_=aq_d[:, 0:w], func=AF.Sign,
                                            bias=np.float32(0.5), scale=1.0),
            [k(aq_d)], [k(aq_s2)])
        dve(lambda e, w=width: e.tensor_scalar(out=aq_u[:, 0:w],
                                               in0=aq_i8[:, 0:w], scalar1=2.0,
                                               scalar2=None, op0=OP.mult),
            [k(aq_i8)], [k(aq_u)])
        dve(lambda e, w=width: e.tensor_tensor(out=aq_s1[:, 0:w],
                                               in0=aq_s1[:, 0:w],
                                               in1=aq_s2[:, 0:w], op=OP.add),
            [k(aq_s1), k(aq_s2)], [k(aq_s1)])
        dve(lambda e, s=src_ap, w=width: e.tensor_tensor(
            out=s, in0=aq_u[:, 0:w], in1=aq_s1[:, 0:w], op=OP.add),
            [k(aq_u), k(aq_s1)], [src_key])

    for hh in range(2 if phases >= 6 else 0):
        co = hh * WW
        absfold(lambda i: (f1v(i)[:, co:co + WW], f1k(i)), NFC, WW)
        scale_rows(C1, WW, TPW, [("g", egB, k(egB)), ("u", euB, k(euB))])
        for c in range(NFC):
            round_chunk(f1v(c)[:, co:co + WW], f1k(c), WW)
        if TAP == f"f1q{hh}":
            tap(f1v(0)[:, co:co + WW], f1k(0), WW)
        if TAP == f"scB{hh}":
            tap(scB[:, 0:WW], k(scB), WW)
        for og in range(I_FFN // 512 if phases >= 7 else 0):
            for fc in range(NFC):
                dma("sync", lambda e, f=fc, o=og: e.dma_start_transpose(
                    out=wT[f][:], in_=tern["g"][o * 512:(o + 1) * 512,
                                                f * 128:(f + 1) * 128]),
                    ["tern_g"], [k(wT[fc])])
            for os_ in range(4):
                for fc in range(NFC):
                    pe(lambda e, f=fc, s=os_, c0=co: e.matmul(
                        pb[0][:, 0:WW], wT[f][:, s * 128:(s + 1) * 128],
                        f1v(f)[:, c0:c0 + WW], start=(f == 0),
                        stop=(f == NFC - 1)),
                       [k(wT[fc]), f1k(fc), "pb0"], ["pb0"],
                       inc=(fc == NFC - 1))
                dve(lambda e, s=os_: e.tensor_tensor(
                    out=gf4[:, s * WW:(s + 1) * WW], in0=pb[0][:, 0:WW],
                    in1=egB[:, 0:WW], op=OP.mult),
                    ["pb0", k(egB)], [k(gf4)])
            for fc in range(NFC):
                dma("scalar", lambda e, f=fc, o=og: e.dma_start_transpose(
                    out=wT[f][:], in_=tern["u"][o * 512:(o + 1) * 512,
                                                f * 128:(f + 1) * 128]),
                    ["tern_u"], [k(wT[fc])])
            for os_ in range(4):
                oc = og * 4 + os_
                for fc in range(NFC):
                    pe(lambda e, f=fc, s=os_, c0=co: e.matmul(
                        pb[1][:, 0:WW], wT[f][:, s * 128:(s + 1) * 128],
                        f1v(f)[:, c0:c0 + WW], start=(f == 0),
                        stop=(f == NFC - 1)),
                       [k(wT[fc]), f1k(fc), "pb1"], ["pb1"],
                       inc=(fc == NFC - 1))
                act(lambda e, s=os_: e.activation(
                    out=aq_s1[:, 0:WW], in_=gf4[:, s * WW:(s + 1) * WW],
                    func=AF.Silu), [k(gf4)], [k(aq_s1)])
                dve(lambda e: e.tensor_tensor(out=uf[:, 0:WW],
                                              in0=pb[1][:, 0:WW],
                                              in1=euB[:, 0:WW], op=OP.mult),
                    ["pb1", k(euB)], [k(uf)])
                dve(lambda e, c=oc: e.tensor_tensor(
                    out=hv(c), in0=aq_s1[:, 0:WW], in1=uf[:, 0:WW],
                    op=OP.mult),
                    [k(aq_s1), k(uf)], [hk(oc)])
        if TAP == f"h0_{hh}":
            tap(hv(0), hk(0), WW)
        if phases >= 8:
            bfly(lambda c: (hv(c), hk(c)), NIC, WW)
        for c in range(NIC if phases >= 8 else 0):
            b = 3 + (c % 2)
            pe(lambda e, cc=c, bb=b: e.matmul(
                pb[bb][:, 0:WW], c_h128[:], hv(cc), start=True, stop=True),
               [k(c_h128), hk(c), f"pb{b}"], [f"pb{b}"])
            dve(lambda e, cc=c, bb=b: e.tensor_copy(out=hv(cc),
                                                    in_=pb[bb][:, 0:WW]),
                [f"pb{b}"], [hk(c)])
        if TAP == f"h0f_{hh}":
            tap(hv(0), hk(0), WW)
        if phases >= 8:
            absfold(lambda i: (hv(i), hk(i)), NIC, WW)
            scale_rows(C2, WW, TPW, [("d", egB, k(egB))])
            for c in range(NIC):
                round_chunk(hv(c), hk(c), WW)
        if TAP == f"h0q_{hh}":
            tap(hv(0), hk(0), WW)
        if TAP == f"h63q_{hh}":
            tap(hv(63), hk(63), WW)
        if TAP == f"h35q_{hh}":
            tap(hv(35), hk(35), WW)
        if TAP == f"scB2_{hh}":
            tap(scB[:, 0:WW], k(scB), WW)
        if TAP == f"edB_{hh}":
            tap(egB[:, 0:WW], k(egB), WW)
        for rnd in range(4 if phases >= 9 else 0):
            for ic in range(NIC):
                wt = wTd[ic % 2]
                dma("sync" if ic % 2 == 0 else "scalar",
                    lambda e, i=ic, r=rnd, d=wt: e.dma_start_transpose(
                        out=d[:], in_=tern["d"][r * 512:(r + 1) * 512,
                                                i * 128:(i + 1) * 128]),
                    ["tern_d"], [k(wt)])
                for ol in range(4):
                    pe(lambda e, i=ic, o=ol, d=wt: e.matmul(
                        pb[o][:, 0:WW], d[:, o * 128:(o + 1) * 128], hv(i),
                        start=(i == 0), stop=(i == NIC - 1)),
                       [k(wt), hk(ic), f"pb{ol}"], [f"pb{ol}"],
                       inc=(ol == 3))
            for ol in range(4):
                oc = rnd * 4 + ol
                dve(lambda e, o=ol: e.tensor_tensor(
                    out=gf4[:, o * WW:(o + 1) * WW], in0=pb[o][:, 0:WW],
                    in1=egB[:, 0:WW], op=OP.mult),
                    [f"pb{ol}", k(egB)], [k(gf4)])
                dma("gpsimd", lambda e, o=ol, c=oc: e.dma_start(
                    out=dsp[c * 128:(c + 1) * 128, 0:WW],
                    in_=gf4[:, o * WW:(o + 1) * WW]),
                    [k(gf4)], ["dsp"], dma_group="dsp")
        for ti in range(TPW if phases >= 9 else 0):
            t = hh * TPW + ti
            for og2 in range(4):
                dma("sync", lambda e, o=og2, i=ti: e.dma_start_transpose(
                    out=ytm[:, o * 512:(o + 1) * 512],
                    in_=dsp[o * 512:(o + 1) * 512, i * 128:(i + 1) * 128]),
                    ["dsp"], [k(ytm)], dma_group="ytm_ld")
            if TAP == f"dtm_{hh}_{ti}":
                tap(ytm[:, 0:D], k(ytm), D)
            ln(ytm[:], k(ytm), ybuf[:], k(ybuf))
            if TAP == f"ln4_{hh}_{ti}":
                tap(ybuf[:, 0:D], k(ybuf), D)
            xt = xst[ti % 2]
            dma("sync", lambda e, T=t, d=xt: e.dma_start(
                out=d[:], in_=x1sp[T * 128:(T + 1) * 128, :]),
                ["x1sp"], [k(xt)])
            ob = outbuf[ti % 2]
            dve(lambda e, d=xt, o=ob: e.tensor_tensor(
                out=o[:, 0:D], in0=d[:], in1=ybuf[:], op=OP.add),
                [k(xt), k(ybuf)], [k(ob)])
            dma("sync", lambda e, T=t, o=ob: e.dma_start(
                out=out_h[T * 128:(T + 1) * 128, :], in_=o[:, 0:D]),
                [k(ob)], ["out_h"])

    if "dqh:out_h" in tk.sems:
        out_sem = tk.sems["dqh:out_h"]
        out_final = tk.cnt["dqh:out_h"]
        tk.raw("sync", lambda e, v=out_final, s=out_sem: e.wait_ge(s, v))
        tk.raw("gpsimd", lambda e, v=out_final, s=out_sem: e.wait_ge(s, v))

    with nc.Block() as block:
        @block.sync
        def _(sync):
            for f in tk.ops["sync"]:
                f(sync)

        @block.scalar
        def _(scalar):
            for f in tk.ops["scalar"]:
                f(scalar)

        @block.vector
        def _(vector):
            for f in tk.ops["vector"]:
                f(vector)

        @block.tensor
        def _(tensor):
            for f in tk.ops["tensor"]:
                f(tensor)

        @block.gpsimd
        def _(gpsimd):
            for f in tk.ops["gpsimd"]:
                f(gpsimd)

    es.close()
    build.stats = (tk.ninst, tk.nwaits, {s: c for s, c in tk.cnt.items() if not s.startswith("dq")})
    return nc


def make_consts():
    h128 = _hadamard(128).astype(ml_dtypes.bfloat16)
    sel = np.zeros((32, 16 * 128), dtype=ml_dtypes.bfloat16)
    for j in range(16):
        sel[2 * j, j * 128:j * 128 + 64] = 1
        sel[2 * j + 1, j * 128 + 64:(j + 1) * 128] = 1
    return {
        "identb": np.eye(128, dtype=ml_dtypes.bfloat16),
        "identf": np.eye(128, dtype=np.float32),
        "h128": np.asarray(h128),
        "ones_cb": np.ones((128, 1), dtype=ml_dtypes.bfloat16),
        "ones_rb": np.ones((1, 128), dtype=ml_dtypes.bfloat16),
        "ones_rf": np.ones((1, 128), dtype=np.float32),
        "ones_cf": np.ones((128, 1), dtype=np.float32),
        "selh": np.asarray(sel),
    }


def kernel(L0_attn_hidden_states, L0_attn_Wq, L0_attn_Wk, L0_attn_Wv,
           L0_attn_Wo, L0_attn_bo, L0_ffn_w_gate, L0_ffn_w_up, L0_ffn_w_down):
    x = np.ascontiguousarray(np.asarray(L0_attn_hidden_states, dtype=np.float32))
    B, S, _ = x.shape
    consts = make_consts()
    shared = {
        "wq": np.ascontiguousarray(L0_attn_Wq, dtype=np.float32),
        "wk": np.ascontiguousarray(L0_attn_Wk, dtype=np.float32),
        "wv": np.ascontiguousarray(L0_attn_Wv, dtype=np.float32),
        "wo": np.ascontiguousarray(L0_attn_Wo, dtype=np.float32),
        "bo": np.ascontiguousarray(
            np.asarray(L0_attn_bo, dtype=np.float32).reshape(1, D)),
        "wg": np.ascontiguousarray(L0_ffn_w_gate, dtype=np.float32),
        "wu": np.ascontiguousarray(L0_ffn_w_up, dtype=np.float32),
        "wd": np.ascontiguousarray(L0_ffn_w_down, dtype=np.float32),
        **consts,
    }
    nc = build(own_tiles=8, all_tiles=16)
    in_maps = []
    half = S // 2
    for c in range(8):
        b, hf = c // 2, c % 2
        own = x[b, hf * half:(hf + 1) * half, :]
        other = x[b, (1 - hf) * half:(2 - hf) * half, :]
        xb_c = np.ascontiguousarray(np.concatenate([own, other], axis=0))
        in_maps.append({"xb": xb_c, **shared})
    res = run_bass_kernel_spmd(nc, in_maps, core_ids=list(range(8)))
    out = np.empty((B, S, D), dtype=np.float32)
    for c in range(8):
        b, hf = c // 2, c % 2
        out[b, hf * half:(hf + 1) * half, :] = np.asarray(res.results[c]["out"])
    return out
